# version 03
# baseline (speedup 1.0000x reference)
import numpy as np
import concourse.bass as bass
import concourse.mybir as mybir
from concourse.bass_utils import run_bass_kernel_spmd
from contextlib import ExitStack

F32 = mybir.dt.float32
BF16 = mybir.dt.bfloat16
I8 = mybir.dt.int8
AF = mybir.ActivationFunctionType
ALU = mybir.AluOpType
AX = mybir.AxisListType

L = 16384
D = 2048
T = 512
NCH = L // T
G = 4
DS = 512
NJ = 4
NDT = 16
NST = 16
NQ = 4
EPS = 1e-5
TQ = 4096
FTC = TQ // T
ISP = 16
NIT = 64
INTER = 8192

BDT_, ALOG_, DPAR_, SG_, SU_, SD_, MASK_ = 0, 4, 68, 72, 136, 200, 216
EPS_, ONE_, NEG1_ = 220, 221, 222
CPW = 224


class Step:
    __slots__ = ("eng", "fn", "inc", "dma", "cc_wait", "cc_inc")

    def __init__(self, eng, fn, inc, dma, cc_wait, cc_inc):
        self.eng, self.fn, self.inc = eng, fn, inc
        self.dma, self.cc_wait, self.cc_inc = dma, cc_wait, cc_inc


def build_program():
    nc = bass.Bass("TRN2", target_bir_lowering=False, debug=False)

    xb = nc.dram_tensor("xb", [L, D], F32, kind="ExternalInput").ap()
    xbs = nc.dram_tensor("xbs", [L, DS], F32, kind="ExternalInput").ap()
    xq = nc.dram_tensor("xq", [TQ, D], F32, kind="ExternalInput").ap()
    wxpT = nc.dram_tensor("wxpT", [D, 96], F32, kind="ExternalInput").ap()
    wdtT = nc.dram_tensor("wdtT", [64, DS], F32, kind="ExternalInput").ap()
    cpack = nc.dram_tensor("cpack", [128, CPW], F32, kind="ExternalInput").ap()
    idn = nc.dram_tensor("idn", [128, 128], F32, kind="ExternalInput").ap()
    wg = nc.dram_tensor("wg", [D, INTER], I8, kind="ExternalInput").ap()
    wu = nc.dram_tensor("wu", [D, INTER], I8, kind="ExternalInput").ap()
    wdp = nc.dram_tensor("wdp", [16, 128, INTER], I8, kind="ExternalInput").ap()
    outq = nc.dram_tensor("outq", [TQ, D], F32, kind="ExternalOutput").ap()

    bcd = nc.dram_tensor("bcd", [32, T], F32, kind="Internal")
    s_d = nc.dram_tensor("s_d", [L, DS], F32, kind="Internal")
    s_g = nc.dram_tensor("s_g", [4 * L, DS], F32, kind="Internal")
    fuse_d = nc.dram_tensor("fuse_d", [INTER, TQ], BF16, kind="Internal")

    steps = []

    def step(eng, n_dma=0, cc_wait=False, cc_inc=False):
        def deco(fn):
            inc = 0 if cc_inc else (16 * n_dma if n_dma else 1)
            steps.append(Step(eng, fn, inc, n_dma > 0, cc_wait, cc_inc))
        return deco

    with ExitStack() as top:
        sbuf = lambda n, s, d=F32: top.enter_context(nc.sbuf_tensor(n, s, d))
        c_wxpT_f = sbuf("c_wxpT_f", [128, NDT * 96])
        c_wxpT = sbuf("c_wxpT", [128, NDT * 96], BF16)
        c_wdtT_f = sbuf("c_wdtT_f", [64, DS])
        c_wdtT = sbuf("c_wdtT", [64, DS], BF16)
        c_pack = sbuf("c_pack", [128, CPW])
        c_acol = sbuf("c_acol", [128, 64])
        c_idn = sbuf("c_idn", [128, 128])

        psA = top.enter_context(nc.psum_tensor("psA", [128, 512], F32))
        psB = top.enter_context(nc.psum_tensor("psB", [128, 512], F32))
        psC = top.enter_context(nc.psum_tensor("psC", [128, 512], F32))
        psE = top.enter_context(nc.psum_tensor("psE", [96, 512], F32))

        semT = top.enter_context(nc.semaphore("semT"))
        semC = top.enter_context(nc.semaphore("semC"))

        @step("sync", n_dma=4)
        def _(e):
            return [
                e.dma_start(out=c_wxpT_f[:, :].rearrange("p (j e) -> p j e", j=NDT),
                            in_=wxpT.rearrange("(j p) e -> p j e", p=128)),
                e.dma_start(out=c_wdtT_f[:, :], in_=wdtT),
                e.dma_start(out=c_pack[:, :], in_=cpack),
                e.dma_start(out=c_idn[:, :], in_=idn),
            ]

        @step("scalar")
        def _(e):
            e.activation(out=c_wxpT[:, :], in_=c_wxpT_f[:, :], func=AF.Copy)
            e.activation(out=c_wdtT[:, :], in_=c_wdtT_f[:, :], func=AF.Copy)
            return e.activation(out=c_acol[:, :], in_=c_pack[:, ALOG_:ALOG_ + 64],
                                func=AF.Exp)

        with ExitStack() as ms:
            msb = lambda n, s, d=F32: ms.enter_context(nc.sbuf_tensor(n, s, d))
            xch = msb("xch", [128, G * D])
            scr = msb("scr", [128, D])
            st = msb("st", [128, 32])
            xnbf = msb("xnbf", [128, G * D], BF16)
            xnTbf = msb("xnTbf", [128, NDT * T], BF16)
            xbsch = msb("xbsch", [128, G * DS])
            xnft = msb("xnft", [128, G * DS])
            xnfT = msb("xnfT", [128, NJ * T])
            dtrawbf = msb("dtrawbf", [64, T], BF16)
            bcsb = msb("bcsb", [32, T])
            zb = msb("zb", [128, T])
            ebuf = msb("ebuf", [128, T])
            dt4 = msb("dt4", [128, NJ * T])
            dtx4 = msb("dtx4", [128, NJ * T])
            brep = msb("brep", [128, NQ * T])
            crep = msb("crep", [128, NQ * T])
            aq = msb("aq", [128, NQ * T])
            uq = msb("uq", [128, NQ * T])
            hq = msb("hq", [128, NQ * T])
            tr1 = msb("tr1", [128, T])
            tr2 = msb("tr2", [128, T])
            yb = msb("yb", [128, NJ * T])
            hst = msb("hst", [128, 64])
            sTt = msb("sTt", [128, G * DS])

            xch3 = xch[:, :].rearrange("p (g d) -> p g d", g=G)
            xnbf3 = xnbf[:, :].rearrange("p (g d) -> p g d", g=G)
            xbsch3 = xbsch[:, :].rearrange("p (g d) -> p g d", g=G)
            xnft3 = xnft[:, :].rearrange("p (g d) -> p g d", g=G)
            brep3 = brep[:, :].rearrange("p (k t) -> p k t", k=NQ)
            crep3 = crep[:, :].rearrange("p (k t) -> p k t", k=NQ)
            aq3 = aq[:, :].rearrange("p (k t) -> p k t", k=NQ)
            uq3 = uq[:, :].rearrange("p (k t) -> p k t", k=NQ)
            hq3 = hq[:, :].rearrange("p (k t) -> p k t", k=NQ)
            xnT4 = xnTbf[:, :].rearrange("p (j gg t) -> p j gg t", j=NDT, gg=G)

            @step("vector")
            def _(e):
                e.tensor_scalar_mul(c_acol[:, :], c_acol[:, :], -1.0)
                return e.memset(hst[:, :], 0.0)

            for c in range(NCH):
                @step("sync", n_dma=2)
                def _(e, c=c):
                    return [
                        e.dma_start(out=xch3,
                                    in_=xb[c * T:(c + 1) * T, :]
                                    .rearrange("(g p) d -> p g d", p=128)),
                        e.dma_start(out=xbsch3,
                                    in_=xbs[c * T:(c + 1) * T, :]
                                    .rearrange("(g p) d -> p g d", p=128)),
                    ]

                for g in range(G):
                    @step("scalar")
                    def _(e, g=g):
                        return e.activation(out=scr[:, :], in_=xch3[:, g, :],
                                            func=AF.Square)

                    @step("vector")
                    def _(e, g=g):
                        e.reduce_sum(st[:, g:g + 1], xch3[:, g, :], axis=AX.X)
                        return e.reduce_sum(st[:, 4 + g:5 + g], scr[:, :],
                                            axis=AX.X)

                @step("vector")
                def _(e):
                    e.tensor_scalar_mul(st[:, 8:12], st[:, 0:4], 1.0 / D)
                    return e.tensor_scalar_mul(st[:, 12:16], st[:, 4:8], 1.0 / D)

                @step("vector")
                def _(e):
                    return e.tensor_tensor(out=st[:, 24:28], in0=st[:, 8:12],
                                           in1=st[:, 8:12], op=ALU.mult)

                @step("vector")
                def _(e):
                    return e.tensor_tensor(out=st[:, 12:16], in0=st[:, 12:16],
                                           in1=st[:, 24:28], op=ALU.subtract)

                @step("scalar")
                def _(e):
                    return e.activation(out=st[:, 24:28], in_=st[:, 12:16],
                                        func=AF.Sqrt,
                                        bias=c_pack[:, EPS_:EPS_ + 1])

                @step("vector")
                def _(e):
                    return e.reciprocal(st[:, 16:20], st[:, 24:28])

                @step("vector")
                def _(e):
                    return e.tensor_tensor(out=st[:, 20:24], in0=st[:, 8:12],
                                           in1=st[:, 16:20], op=ALU.mult)

                @step("vector")
                def _(e):
                    return e.tensor_scalar_mul(st[:, 20:24], st[:, 20:24], -1.0)

                @step("scalar")
                def _(e):
                    r = None
                    for g in range(G):
                        e.activation(out=xnbf3[:, g, :], in_=xch3[:, g, :],
                                     func=AF.Identity, scale=st[:, 16 + g:17 + g],
                                     bias=st[:, 20 + g:21 + g])
                        r = e.activation(out=xnft3[:, g, :], in_=xbsch3[:, g, :],
                                         func=AF.Identity, scale=st[:, 16 + g:17 + g],
                                         bias=st[:, 20 + g:21 + g])
                    return r

                @step("sync", n_dma=4)
                def _(e):
                    return [
                        e.dma_start(out=xnT4[:, :, g, :],
                                    in_=xnbf3[:, g, :]
                                    .rearrange("t (j p) -> t j p", j=NDT),
                                    transpose=True)
                        for g in range(G)
                    ]

                for j in range(NJ):
                    @step("tensor")
                    def _(e, j=j):
                        r = None
                        for g in range(G):
                            r = e.transpose(psA[:, g * 128:(g + 1) * 128],
                                            xnft3[:, g, j * 128:(j + 1) * 128],
                                            c_idn[:, :])
                        return r

                    @step("vector")
                    def _(e, j=j):
                        return e.tensor_copy(out=xnfT[:, j * T:(j + 1) * T],
                                             in_=psA[:, :])

                @step("tensor")
                def _(e):
                    r = None
                    for j2 in range(NDT):
                        r = e.matmul(psE[:, :],
                                     c_wxpT[:, j2 * 96:(j2 + 1) * 96],
                                     xnTbf[:, j2 * T:(j2 + 1) * T],
                                     start=(j2 == 0), stop=(j2 == NDT - 1))
                    return r

                @step("scalar")
                def _(e):
                    return e.activation(out=dtrawbf[:, :], in_=psE[0:64, :],
                                        func=AF.Copy)

                @step("vector")
                def _(e):
                    return e.tensor_copy(out=bcsb[:, :], in_=psE[64:96, :])

                @step("sync", n_dma=1)
                def _(e):
                    return e.dma_start(out=bcd.ap(), in_=bcsb[:, :])

                for j in range(NJ):
                    @step("tensor")
                    def _(e, j=j):
                        return e.matmul(psB[:, :],
                                        c_wdtT[:, j * 128:(j + 1) * 128],
                                        dtrawbf[:, :], start=True, stop=True)

                    @step("scalar")
                    def _(e, j=j):
                        return e.activation(out=zb[:, :], in_=psB[:, :],
                                            func=AF.Identity,
                                            bias=c_pack[:, BDT_ + j:BDT_ + j + 1])

                    @step("scalar")
                    def _(e):
                        e.activation(out=ebuf[:, :], in_=zb[:, :], func=AF.Abs)
                        e.activation(out=ebuf[:, :], in_=ebuf[:, :], func=AF.Exp,
                                     scale=c_pack[:, NEG1_:NEG1_ + 1])
                        return e.activation(out=ebuf[:, :], in_=ebuf[:, :],
                                            func=AF.Ln,
                                            bias=c_pack[:, ONE_:ONE_ + 1])

                    @step("vector")
                    def _(e, j=j):
                        e.scalar_tensor_tensor(out=dt4[:, j * T:(j + 1) * T],
                                               in0=zb[:, :], scalar=0.0,
                                               in1=ebuf[:, :],
                                               op0=ALU.max, op1=ALU.add)
                        return e.tensor_tensor(out=dtx4[:, j * T:(j + 1) * T],
                                               in0=dt4[:, j * T:(j + 1) * T],
                                               in1=xnfT[:, j * T:(j + 1) * T],
                                               op=ALU.mult)

                @step("vector")
                def _(e):
                    return e.memset(yb[:, :], 0.0)

                for nq in range(NQ):
                    @step("sync", n_dma=2)
                    def _(e, nq=nq):
                        return [
                            e.dma_start(out=brep3,
                                        in_=bcd.ap()[nq * 4:(nq + 1) * 4, :]
                                        .unsqueeze(0).broadcast_to([128, 4, T])),
                            e.dma_start(out=crep3,
                                        in_=bcd.ap()[16 + nq * 4:20 + nq * 4, :]
                                        .unsqueeze(0).broadcast_to([128, 4, T])),
                        ]

                    for j in range(NJ):
                        @step("scalar")
                        def _(e, nq=nq, j=j):
                            r = None
                            for k in range(4):
                                idx = j * NST + nq * 4 + k
                                r = e.activation(out=aq3[:, k, :],
                                                 in_=dt4[:, j * T:(j + 1) * T],
                                                 func=AF.Exp,
                                                 scale=c_acol[:, idx:idx + 1])
                            return r

                        @step("vector")
                        def _(e, nq=nq, j=j):
                            e.tensor_tensor(
                                out=uq3,
                                in0=dtx4[:, j * T:(j + 1) * T].unsqueeze(1)
                                .broadcast_to([128, 4, T]),
                                in1=brep3, op=ALU.mult)
                            r = None
                            for k in range(4):
                                idx = j * NST + nq * 4 + k
                                r = e.tensor_tensor_scan(
                                    out=hq3[:, k, :], data0=aq3[:, k, :],
                                    data1=uq3[:, k, :],
                                    initial=hst[:, idx:idx + 1],
                                    op0=ALU.mult, op1=ALU.add)
                            return r

                        @step("vector")
                        def _(e, nq=nq, j=j):
                            base = j * NST + nq * 4
                            return e.tensor_copy(out=hst[:, base:base + 4],
                                                 in_=hq3[:, :, T - 1:T])

                        @step("vector")
                        def _(e, nq=nq, j=j):
                            e.tensor_tensor(out=aq[:, :], in0=hq[:, :],
                                            in1=crep[:, :], op=ALU.mult)
                            e.tensor_tensor(out=tr1[:, :], in0=aq3[:, 0, :],
                                            in1=aq3[:, 1, :], op=ALU.add)
                            e.tensor_tensor(out=tr2[:, :], in0=aq3[:, 2, :],
                                            in1=aq3[:, 3, :], op=ALU.add)
                            e.tensor_tensor(out=tr1[:, :], in0=tr1[:, :],
                                            in1=tr2[:, :], op=ALU.add)
                            return e.tensor_tensor(
                                out=yb[:, j * T:(j + 1) * T],
                                in0=yb[:, j * T:(j + 1) * T],
                                in1=tr1[:, :], op=ALU.add)

                @step("vector")
                def _(e):
                    r = None
                    for j in range(NJ):
                        r = e.scalar_tensor_tensor(
                            out=yb[:, j * T:(j + 1) * T],
                            in0=xnfT[:, j * T:(j + 1) * T],
                            scalar=c_pack[:, DPAR_ + j:DPAR_ + j + 1],
                            in1=yb[:, j * T:(j + 1) * T],
                            op0=ALU.mult, op1=ALU.add)
                    return r

                for g in range(G):
                    @step("tensor")
                    def _(e, g=g):
                        r = None
                        for jj in range(NJ):
                            r = e.transpose(
                                psA[:, jj * 128:(jj + 1) * 128],
                                yb[:, jj * T + g * 128:jj * T + (g + 1) * 128],
                                c_idn[:, :])
                        return r

                    @step("vector")
                    def _(e, g=g):
                        return e.tensor_copy(out=sTt[:, g * DS:(g + 1) * DS],
                                             in_=psA[:, :])

                @step("sync", n_dma=1)
                def _(e, c=c):
                    return e.dma_start(
                        out=s_d.ap()[c * T:(c + 1) * T, :]
                        .rearrange("(g p) d -> p g d", p=128),
                        in_=sTt[:, :].rearrange("p (g d) -> p g d", g=G))

        @step("gpsimd", cc_inc=True)
        def _(e):
            return e.collective_compute(
                "AllGather", ALU.bypass,
                replica_groups=[[0, 1, 2, 3], [4, 5, 6, 7]],
                ins=[s_d.ap().opt()], outs=[s_g.ap().opt()])

        with ExitStack() as fa:
            fsb = lambda n, s, d=F32: fa.enter_context(nc.sbuf_tensor(n, s, d))
            fxch = fsb("fxch", [128, G * D])
            fscr = fsb("fscr", [128, D])
            fst = fsb("fst", [128, 32])
            fxnbf = fsb("fxnbf", [128, G * D], BF16)
            xnT = fsb("xnT", [128, NDT * T], BF16)
            wgb = fsb("wgb", [128, NDT * 512], BF16)
            wub = fsb("wub", [128, NDT * 512], BF16)
            gb = fsb("gb", [128, 512])
            fusebf = fsb("fusebf", [128, 512], BF16)

            fxch3 = fxch[:, :].rearrange("p (g d) -> p g d", g=G)
            fxnbf3 = fxnbf[:, :].rearrange("p (g d) -> p g d", g=G)
            fxnT4 = xnT[:, :].rearrange("p (j gg t) -> p j gg t", j=NDT, gg=G)
            wgb3 = wgb[:, :].rearrange("p (j i) -> p j i", j=NDT)
            wub3 = wub[:, :].rearrange("p (j i) -> p j i", j=NDT)

            for tc in range(FTC):
                @step("sync", n_dma=1)
                def _(e, tc=tc):
                    return e.dma_start(out=fxch3,
                                       in_=xq[tc * T:(tc + 1) * T, :]
                                       .rearrange("(g p) d -> p g d", p=128))

                for g in range(G):
                    @step("scalar")
                    def _(e, g=g):
                        return e.activation(out=fscr[:, :], in_=fxch3[:, g, :],
                                            func=AF.Square)

                    @step("vector")
                    def _(e, g=g):
                        e.reduce_sum(fst[:, g:g + 1], fxch3[:, g, :], axis=AX.X)
                        return e.reduce_sum(fst[:, 4 + g:5 + g], fscr[:, :],
                                            axis=AX.X)

                @step("vector")
                def _(e):
                    e.tensor_scalar_mul(fst[:, 8:12], fst[:, 0:4], 1.0 / D)
                    return e.tensor_scalar_mul(fst[:, 12:16], fst[:, 4:8], 1.0 / D)

                @step("vector")
                def _(e):
                    return e.tensor_tensor(out=fst[:, 24:28], in0=fst[:, 8:12],
                                           in1=fst[:, 8:12], op=ALU.mult)

                @step("vector")
                def _(e):
                    return e.tensor_tensor(out=fst[:, 12:16], in0=fst[:, 12:16],
                                           in1=fst[:, 24:28], op=ALU.subtract)

                @step("scalar")
                def _(e):
                    return e.activation(out=fst[:, 24:28], in_=fst[:, 12:16],
                                        func=AF.Sqrt,
                                        bias=c_pack[:, EPS_:EPS_ + 1])

                @step("vector")
                def _(e):
                    return e.reciprocal(fst[:, 16:20], fst[:, 24:28])

                @step("vector")
                def _(e):
                    return e.tensor_tensor(out=fst[:, 20:24], in0=fst[:, 8:12],
                                           in1=fst[:, 16:20], op=ALU.mult)

                @step("vector")
                def _(e):
                    return e.tensor_scalar_mul(fst[:, 20:24], fst[:, 20:24], -1.0)

                @step("scalar")
                def _(e):
                    r = None
                    for g in range(G):
                        r = e.activation(out=fxnbf3[:, g, :], in_=fxch3[:, g, :],
                                         func=AF.Identity, scale=fst[:, 16 + g:17 + g],
                                         bias=fst[:, 20 + g:21 + g])
                    return r

                @step("sync", n_dma=4)
                def _(e):
                    return [
                        e.dma_start(out=fxnT4[:, :, g, :],
                                    in_=fxnbf3[:, g, :]
                                    .rearrange("t (j p) -> t j p", j=NDT),
                                    transpose=True)
                        for g in range(G)
                    ]

                for sp in range(ISP):
                    @step("gpsimd", n_dma=2)
                    def _(e, sp=sp):
                        return [
                            e.dma_start(out=wgb3,
                                        in_=wg[:, sp * 512:(sp + 1) * 512]
                                        .rearrange("(j p) i -> p j i", p=128)),
                            e.dma_start(out=wub3,
                                        in_=wu[:, sp * 512:(sp + 1) * 512]
                                        .rearrange("(j p) i -> p j i", p=128)),
                        ]

                    for it in range(4):
                        @step("tensor")
                        def _(e, it=it):
                            r = None
                            for j in range(NDT):
                                e.matmul(psA[:, :],
                                         wgb3[:, j, it * 128:(it + 1) * 128],
                                         xnT[:, j * T:(j + 1) * T],
                                         start=(j == 0), stop=(j == NDT - 1))
                            for j in range(NDT):
                                r = e.matmul(psB[:, :],
                                             wub3[:, j, it * 128:(it + 1) * 128],
                                             xnT[:, j * T:(j + 1) * T],
                                             start=(j == 0), stop=(j == NDT - 1))
                            return r

                        @step("scalar")
                        def _(e, sp=sp, it=it):
                            col = SG_ + sp * 4 + it
                            return e.activation(out=gb[:, :], in_=psA[:, :],
                                                func=AF.Silu,
                                                scale=c_pack[:, col:col + 1])

                        @step("vector")
                        def _(e, sp=sp, it=it):
                            col = SU_ + sp * 4 + it
                            return e.scalar_tensor_tensor(
                                out=fusebf[:, :], in0=psB[:, :],
                                scalar=c_pack[:, col:col + 1], in1=gb[:, :],
                                op0=ALU.mult, op1=ALU.mult)

                        @step("sync", n_dma=1)
                        def _(e, sp=sp, it=it, tc=tc):
                            row = (sp * 4 + it) * 128
                            return e.dma_start(
                                out=fuse_d.ap()[row:row + 128,
                                                tc * T:(tc + 1) * T],
                                in_=fusebf[:, :])

        with ExitStack() as fb:
            fsb2 = lambda n, s, d=F32: fb.enter_context(nc.sbuf_tensor(n, s, d))
            fusesb = fsb2("fusesb", [128, NIT * T], BF16)
            wdb = fsb2("wdb", [128, NIT * 128], BF16)
            gbuf = fsb2("gbuf", [128, NDT * T])
            gTt = fsb2("gTt", [128, D])
            ssel = fsb2("ssel", [128, 16 * DS])
            sacc = fsb2("sacc", [128, D])
            xt = fsb2("xt", [128, D])
            osb = fsb2("osb", [128, D])

            fusesb3 = fusesb[:, :].rearrange("p (it t) -> p it t", it=NIT)
            wdb3 = wdb[:, :].rearrange("p (it d) -> p it d", it=NIT)
            gbuf3 = gbuf[:, :].rearrange("p (j t) -> p j t", j=NDT)
            ssel4 = ssel[:, :].rearrange("p (r q d) -> p r q d", r=4, q=4)
            sgv = s_g.ap().rearrange("(r q c gg p) d -> p r q c gg d",
                                     r=4, q=4, c=FTC, gg=G, p=128)

            for tc in range(FTC):
                @step("sync", n_dma=1)
                def _(e, tc=tc):
                    return e.dma_start(
                        out=fusesb3,
                        in_=fuse_d.ap()[:, tc * T:(tc + 1) * T]
                        .rearrange("(it p) t -> p it t", p=128))

                for dt in range(NDT):
                    @step("gpsimd", n_dma=1)
                    def _(e, dt=dt):
                        return e.dma_start(out=wdb[:, :], in_=wdp[dt, :, :])

                    @step("tensor")
                    def _(e, dt=dt):
                        r = None
                        for it in range(NIT):
                            r = e.matmul(psC[:, :], wdb3[:, it, :],
                                         fusesb3[:, it, :],
                                         start=(it == 0), stop=(it == NIT - 1))
                        return r

                    @step("scalar")
                    def _(e, dt=dt):
                        col = SD_ + dt
                        return e.activation(out=gbuf3[:, dt, :], in_=psC[:, :],
                                            func=AF.Silu,
                                            scale=c_pack[:, col:col + 1])

                for g in range(G):
                    @step("sync", n_dma=2, cc_wait=True)
                    def _(e, tc=tc, g=g):
                        return [
                            e.dma_start(out=ssel4, in_=sgv[:, :, :, tc, g, :]),
                            e.dma_start(out=xt[:, :],
                                        in_=xq[tc * T + g * 128:
                                               tc * T + (g + 1) * 128, :]),
                        ]

                    for rr in range(4):
                        @step("tensor")
                        def _(e, g=g, rr=rr):
                            r = None
                            for u in range(4):
                                dt = rr * 4 + u
                                r = e.transpose(
                                    psA[:, u * 128:(u + 1) * 128],
                                    gbuf[:, dt * T + g * 128:
                                         dt * T + (g + 1) * 128],
                                    c_idn[:, :])
                            return r

                        @step("vector")
                        def _(e, rr=rr):
                            return e.tensor_copy(
                                out=gTt[:, rr * 512:(rr + 1) * 512],
                                in_=psA[:, :])

                    @step("vector")
                    def _(e):
                        r = None
                        for r4 in range(4):
                            e.tensor_single_scalar(
                                sacc[:, r4 * DS:(r4 + 1) * DS],
                                ssel4[:, r4, 0, :],
                                c_pack[:, MASK_:MASK_ + 1], ALU.mult)
                            for Q in range(1, 4):
                                r = e.scalar_tensor_tensor(
                                    out=sacc[:, r4 * DS:(r4 + 1) * DS],
                                    in0=ssel4[:, r4, Q, :],
                                    scalar=c_pack[:, MASK_ + Q:MASK_ + Q + 1],
                                    in1=sacc[:, r4 * DS:(r4 + 1) * DS],
                                    op0=ALU.mult, op1=ALU.add)
                        return r

                    @step("vector")
                    def _(e):
                        e.tensor_tensor(out=osb[:, :], in0=sacc[:, :],
                                        in1=gTt[:, :], op=ALU.mult)
                        return e.tensor_tensor(out=osb[:, :], in0=osb[:, :],
                                               in1=xt[:, :], op=ALU.add)

                    @step("sync", n_dma=1)
                    def _(e, tc=tc, g=g):
                        return e.dma_start(
                            out=outq[tc * T + g * 128:tc * T + (g + 1) * 128, :],
                            in_=osb[:, :])

        total = sum(s.inc for s in steps)

        with nc.Block() as block:
            def make_body(engname):
                def body(eng):
                    cnt = 0
                    for s in steps:
                        if s.eng == engname:
                            if cnt > 0:
                                eng.wait_ge(semT, cnt)
                            if s.cc_wait:
                                eng.wait_ge(semC, 1)
                            ret = s.fn(eng)
                            if s.cc_inc:
                                ret.then_inc(semC, 1)
                            elif s.dma:
                                instrs = ret if isinstance(ret, list) else [ret]
                                assert 16 * len(instrs) == s.inc, (
                                    engname, len(instrs), s.inc)
                                for ins_ in instrs:
                                    ins_.then_inc(semT, 16)
                            else:
                                ret.then_inc(semT, 1)
                        cnt += s.inc
                    eng.wait_ge(semT, total)
                return body

            block.sync(make_body("sync"))
            block.gpsimd(make_body("gpsimd"))
            block.scalar(make_body("scalar"))
            block.vector(make_body("vector"))
            block.tensor(make_body("tensor"))

    return nc


def kernel(L0_attn_x, L0_attn_W_xproj, L0_attn_W_dt, L0_attn_b_dt,
           L0_attn_A_log, L0_attn_D_param, L0_ffn_w_gate, L0_ffn_w_up,
           L0_ffn_w_down, L0_ffn_s_gate, L0_ffn_s_up, L0_ffn_s_down):
    x = np.asarray(L0_attn_x, dtype=np.float32)
    W_xproj = np.asarray(L0_attn_W_xproj, dtype=np.float32)
    W_dt = np.asarray(L0_attn_W_dt, dtype=np.float32)
    b_dt = np.asarray(L0_attn_b_dt, dtype=np.float32)
    A_log = np.asarray(L0_attn_A_log, dtype=np.float32)
    D_param = np.asarray(L0_attn_D_param, dtype=np.float32)
    w_gate = np.asarray(L0_ffn_w_gate)
    w_up = np.asarray(L0_ffn_w_up)
    w_down = np.asarray(L0_ffn_w_down)
    s_gate = np.asarray(L0_ffn_s_gate, dtype=np.float32)
    s_up = np.asarray(L0_ffn_s_up, dtype=np.float32)
    s_down = np.asarray(L0_ffn_s_down, dtype=np.float32)

    nc = build_program()

    wxpT = np.ascontiguousarray(W_xproj.T)
    wg = np.ascontiguousarray(w_gate)
    wu = np.ascontiguousarray(w_up)
    wdp = np.ascontiguousarray(
        w_down.reshape(64, 128, 16, 128).transpose(2, 1, 0, 3)
    ).reshape(16, 128, INTER)
    idn = np.eye(128, dtype=np.float32)
    sg_c = np.ascontiguousarray(s_gate.reshape(64, 128).T)
    su_c = np.ascontiguousarray(s_up.reshape(64, 128).T)
    sd_c = np.ascontiguousarray(s_down.reshape(16, 128).T)

    in_maps = []
    for k in range(8):
        b, r = k // 4, k % 4
        dlo = r * DS
        cp = np.zeros((128, CPW), dtype=np.float32)
        cp[:, EPS_] = EPS
        cp[:, ONE_] = 1.0
        cp[:, NEG1_] = -1.0
        cp[:, BDT_:BDT_ + 4] = b_dt[dlo:dlo + DS].reshape(4, 128).T
        cp[:, ALOG_:ALOG_ + 64] = (
            A_log[dlo:dlo + DS].reshape(4, 128, 16).transpose(1, 0, 2)
            .reshape(128, 64))
        cp[:, DPAR_:DPAR_ + 4] = D_param[dlo:dlo + DS].reshape(4, 128).T
        cp[:, SG_:SG_ + 64] = sg_c
        cp[:, SU_:SU_ + 64] = su_c
        cp[:, SD_:SD_ + 16] = sd_c
        cp[:, MASK_ + r] = 1.0
        in_maps.append(dict(
            xb=np.ascontiguousarray(x[b]),
            xbs=np.ascontiguousarray(x[b][:, dlo:dlo + DS]),
            xq=np.ascontiguousarray(x[b, r * TQ:(r + 1) * TQ]),
            wxpT=wxpT,
            wdtT=np.ascontiguousarray(W_dt[dlo:dlo + DS].T),
            cpack=cp,
            idn=idn,
            wg=wg,
            wu=wu,
            wdp=wdp,
        ))

    res = run_bass_kernel_spmd(nc, in_maps, core_ids=list(range(8)))

    out = np.empty((2, L, D), dtype=np.float32)
    for k in range(8):
        b, r = k // 4, k % 4
        out[b, r * TQ:(r + 1) * TQ] = np.asarray(res.results[k]["outq"])
    return out


# version 04
# speedup vs baseline: 1.3926x; 1.3926x over previous
import numpy as np
import concourse.bass as bass
import concourse.mybir as mybir
from concourse.bass_utils import run_bass_kernel_spmd
from contextlib import ExitStack

F32 = mybir.dt.float32
BF16 = mybir.dt.bfloat16
I8 = mybir.dt.int8
AF = mybir.ActivationFunctionType
ALU = mybir.AluOpType
AX = mybir.AxisListType

L = 16384
D = 2048
T = 512
NCH = L // T
G = 4
DS = 512
NJ = 4
NDT = 16
NST = 16
NQ = 4
EPS = 1e-5
TQ = 4096
FTC = TQ // T
ISP = 16
NIT = 64
INTER = 8192

BDT_, ALOG_, DPAR_, SG_, SU_, SD_, MASK_ = 0, 4, 68, 72, 136, 200, 216
EPS_, ONE_, NEG1_ = 220, 221, 222
CPW = 224

ENGS = ("sync", "gpsimd", "scalar", "vector", "tensor")


class Step:
    __slots__ = ("eng", "fn", "inc", "dma", "r", "w", "cc_wait", "cc_inc",
                 "barrier")

    def __init__(self, eng, fn, inc, dma, r, w, cc_wait, cc_inc, barrier):
        self.eng, self.fn, self.inc, self.dma = eng, fn, inc, dma
        self.r, self.w = r, w
        self.cc_wait, self.cc_inc, self.barrier = cc_wait, cc_inc, barrier


def _ln_steps(step, eng_pfx, XCHT, STT_, SCRT, xchH, stH, scrbuf, c_pack,
              nsub):
    n = nsub

    for gg in range(nsub):
        @step("scalar", r=(XCHT,), w=(SCRT,))
        def _(e, gg=gg):
            return e.activation(out=scrbuf[:, :], in_=xchH[:, gg, :],
                                func=AF.Square)

        @step("vector", r=(XCHT, SCRT), w=(STT_,))
        def _(e, gg=gg):
            e.reduce_sum(stH[:, gg:gg + 1], xchH[:, gg, :], axis=AX.X)
            return e.reduce_sum(stH[:, n + gg:n + gg + 1], scrbuf[:, :],
                                axis=AX.X)

    @step("vector", r=(STT_,), w=(STT_,))
    def _(e):
        e.tensor_scalar_mul(stH[:, 2 * n:3 * n], stH[:, 0:n], 1.0 / D)
        return e.tensor_scalar_mul(stH[:, 3 * n:4 * n], stH[:, n:2 * n],
                                   1.0 / D)

    @step("vector", r=(STT_,), w=(STT_,))
    def _(e):
        return e.tensor_tensor(out=stH[:, 4 * n:5 * n], in0=stH[:, 2 * n:3 * n],
                               in1=stH[:, 2 * n:3 * n], op=ALU.mult)

    @step("vector", r=(STT_,), w=(STT_,))
    def _(e):
        return e.tensor_tensor(out=stH[:, 3 * n:4 * n], in0=stH[:, 3 * n:4 * n],
                               in1=stH[:, 4 * n:5 * n], op=ALU.subtract)

    @step("scalar", r=(STT_, "c_pack"), w=(STT_,))
    def _(e):
        return e.activation(out=stH[:, 4 * n:5 * n], in_=stH[:, 3 * n:4 * n],
                            func=AF.Sqrt, bias=c_pack[:, EPS_:EPS_ + 1])

    @step("vector", r=(STT_,), w=(STT_,))
    def _(e):
        return e.reciprocal(stH[:, 5 * n:6 * n], stH[:, 4 * n:5 * n])

    @step("vector", r=(STT_,), w=(STT_,))
    def _(e):
        return e.tensor_tensor(out=stH[:, 6 * n:7 * n], in0=stH[:, 2 * n:3 * n],
                               in1=stH[:, 5 * n:6 * n], op=ALU.mult)

    @step("vector", r=(STT_,), w=(STT_,))
    def _(e):
        return e.tensor_scalar_mul(stH[:, 6 * n:7 * n], stH[:, 6 * n:7 * n],
                                   -1.0)


def build_program():
    nc = bass.Bass("TRN2", target_bir_lowering=False, debug=False)

    xb = nc.dram_tensor("xb", [L, D], F32, kind="ExternalInput").ap()
    xbs = nc.dram_tensor("xbs", [L, DS], F32, kind="ExternalInput").ap()
    xq = nc.dram_tensor("xq", [TQ, D], F32, kind="ExternalInput").ap()
    wxpT = nc.dram_tensor("wxpT", [D, 96], F32, kind="ExternalInput").ap()
    wdtT = nc.dram_tensor("wdtT", [64, DS], F32, kind="ExternalInput").ap()
    cpack = nc.dram_tensor("cpack", [128, CPW], F32, kind="ExternalInput").ap()
    idn = nc.dram_tensor("idn", [128, 128], F32, kind="ExternalInput").ap()
    wg = nc.dram_tensor("wg", [D, INTER], I8, kind="ExternalInput").ap()
    wu = nc.dram_tensor("wu", [D, INTER], I8, kind="ExternalInput").ap()
    wdp = nc.dram_tensor("wdp", [16, 128, INTER], I8, kind="ExternalInput").ap()
    outq = nc.dram_tensor("outq", [TQ, D], F32, kind="ExternalOutput").ap()

    bcd = nc.dram_tensor("bcd", [2, 32, T], F32, kind="Internal")
    s_d = nc.dram_tensor("s_d", [L, DS], F32, kind="Internal")
    s_g = nc.dram_tensor("s_g", [4 * L, DS], F32, kind="Internal")
    fuse_d = nc.dram_tensor("fuse_d", [INTER, TQ], BF16, kind="Internal")

    steps = []

    def step(eng, r, w, n_dma=0, cc_wait=False, cc_inc=False):
        def deco(fn):
            inc = 0 if cc_inc else (16 * n_dma if n_dma else 1)
            steps.append(Step(eng, fn, inc, n_dma > 0, tuple(r), tuple(w),
                              cc_wait, cc_inc, False))
        return deco

    def barrier():
        for eng in ENGS:
            steps.append(Step(eng, None, 1, False, (), (), False, False, True))

    with ExitStack() as top:
        sbuf = lambda n, s, d=F32: top.enter_context(nc.sbuf_tensor(n, s, d))
        c_wxpT_f = sbuf("c_wxpT_f", [128, NDT * 96])
        c_wxpT = sbuf("c_wxpT", [128, NDT * 96], BF16)
        c_wdtT_f = sbuf("c_wdtT_f", [64, DS])
        c_wdtT = sbuf("c_wdtT", [64, DS], BF16)
        c_pack = sbuf("c_pack", [128, CPW])
        c_acol = sbuf("c_acol", [128, 64])
        c_idn = sbuf("c_idn", [128, 128])

        psT = top.enter_context(nc.psum_tensor("psT", [128, 512], F32))
        psE = top.enter_context(nc.psum_tensor("psE", [96, 512], F32))
        psBm = top.enter_context(nc.psum_tensor("psBm", [128, 512], F32))
        psG = [top.enter_context(nc.psum_tensor(f"psG{i}", [128, 512], F32))
               for i in range(2)]
        psU = [top.enter_context(nc.psum_tensor(f"psU{i}", [128, 512], F32))
               for i in range(2)]
        psC = top.enter_context(nc.psum_tensor("psC", [128, 512], F32))

        semE = {e: top.enter_context(nc.semaphore(f"sem_{e}")) for e in ENGS}
        semC = top.enter_context(nc.semaphore("semC"))

        @step("sync", r=(), w=("c_wxpT_f", "c_wdtT_f", "c_pack", "c_idn"),
              n_dma=4)
        def _(e):
            return [
                e.dma_start(out=c_wxpT_f[:, :].rearrange("p (j e) -> p j e",
                                                         j=NDT),
                            in_=wxpT.rearrange("(j p) e -> p j e", p=128)),
                e.dma_start(out=c_wdtT_f[:, :], in_=wdtT),
                e.dma_start(out=c_pack[:, :], in_=cpack),
                e.dma_start(out=c_idn[:, :], in_=idn),
            ]

        @step("scalar", r=("c_wxpT_f", "c_wdtT_f", "c_pack"),
              w=("c_wxpT", "c_wdtT", "c_acol"))
        def _(e):
            e.activation(out=c_wxpT[:, :], in_=c_wxpT_f[:, :], func=AF.Copy)
            e.activation(out=c_wdtT[:, :], in_=c_wdtT_f[:, :], func=AF.Copy)
            return e.activation(out=c_acol[:, :],
                                in_=c_pack[:, ALOG_:ALOG_ + 64], func=AF.Exp)

        with ExitStack() as ms:
            msb = lambda n, s, d=F32: ms.enter_context(nc.sbuf_tensor(n, s, d))
            xchH = [msb(f"xchH{i}", [128, 2 * D]) for i in range(2)]
            xbschH = [msb(f"xbschH{i}", [128, 2 * DS]) for i in range(2)]
            stH = [msb(f"stH{i}", [128, 16]) for i in range(2)]
            scr = msb("scr", [128, D])
            xnbf = msb("xnbf", [128, G * D], BF16)
            xnTbf = msb("xnTbf", [128, NDT * T], BF16)
            xnsT = msb("xnsT", [128, G * DS])
            xnfT = msb("xnfT", [128, NJ * T])
            dtrawbf = msb("dtrawbf", [64, T], BF16)
            bcsb = msb("bcsb", [32, T])
            zb = msb("zb", [128, T])
            ebuf = msb("ebuf", [128, T])
            dt4 = msb("dt4", [128, NJ * T])
            dtx4 = msb("dtx4", [128, NJ * T])
            brep = [msb(f"brep{i}", [128, NQ * T]) for i in range(2)]
            crep = [msb(f"crep{i}", [128, NQ * T]) for i in range(2)]
            aq = [msb(f"aq{i}", [128, NQ * T]) for i in range(2)]
            uq = msb("uq", [128, NQ * T])
            hq = msb("hq", [128, NQ * T])
            tr1 = msb("tr1", [128, T])
            tr2 = zb
            yb = msb("yb", [128, NJ * T])
            hst = msb("hst", [128, 64])

            view3 = lambda ap_, k: ap_[:, :].rearrange("p (g d) -> p g d", g=k)
            uq3 = view3(uq, NQ)
            hq3 = view3(hq, NQ)
            xnbf3 = view3(xnbf, G)
            xnsT3 = view3(xnsT, G)
            xnT4 = xnTbf[:, :].rearrange("p (j gg t) -> p j gg t", j=NDT, gg=G)

            @step("vector", r=("c_acol",), w=("c_acol", "hst"))
            def _(e):
                e.tensor_scalar_mul(c_acol[:, :], c_acol[:, :], -1.0)
                return e.memset(hst[:, :], 0.0)

            for c in range(NCH):
                p = c % 2
                BCD = f"bcd{p}"

                for hh in range(2):
                    XCHT, XBST, STT_ = f"xchH{hh}", f"xbschH{hh}", f"stH{hh}"
                    xch3 = view3(xchH[hh], 2)
                    xbs3 = view3(xbschH[hh], 2)

                    @step("sync", r=(), w=(XCHT, XBST), n_dma=2)
                    def _(e, c=c, hh=hh, xch3=xch3, xbs3=xbs3):
                        lo = c * T + hh * 256
                        return [
                            e.dma_start(out=xch3,
                                        in_=xb[lo:lo + 256, :]
                                        .rearrange("(g p) d -> p g d", p=128)),
                            e.dma_start(out=xbs3,
                                        in_=xbs[lo:lo + 256, :]
                                        .rearrange("(g p) d -> p g d", p=128)),
                        ]

                    _ln_steps(step, "m", XCHT, STT_, "scr", xch3, stH[hh],
                              scr, c_pack, 2)

                    @step("scalar", r=(XCHT, XBST, STT_), w=("xnbf", "xnsT"))
                    def _(e, hh=hh, xch3=xch3, xbs3=xbs3):
                        r = None
                        for gg in range(2):
                            g = 2 * hh + gg
                            e.activation(out=xnbf3[:, g, :], in_=xch3[:, gg, :],
                                         func=AF.Identity,
                                         scale=stH[hh][:, 10 + gg:11 + gg],
                                         bias=stH[hh][:, 12 + gg:13 + gg])
                            r = e.activation(out=xnsT3[:, g, :],
                                             in_=xbs3[:, gg, :],
                                             func=AF.Identity,
                                             scale=stH[hh][:, 10 + gg:11 + gg],
                                             bias=stH[hh][:, 12 + gg:13 + gg])
                        return r

                @step("sync", r=("xnbf",), w=("xnTbf",), n_dma=4)
                def _(e):
                    return [
                        e.dma_start(out=xnT4[:, :, g, :],
                                    in_=xnbf3[:, g, :]
                                    .rearrange("t (j p) -> t j p", j=NDT),
                                    transpose=True)
                        for g in range(G)
                    ]

                for j in range(NJ):
                    @step("tensor", r=("xnsT", "c_idn"), w=("psT",))
                    def _(e, j=j):
                        r = None
                        for g in range(G):
                            r = e.transpose(psT[:, g * 128:(g + 1) * 128],
                                            xnsT3[:, g, j * 128:(j + 1) * 128],
                                            c_idn[:, :])
                        return r

                    @step("vector", r=("psT",), w=("xnfT",))
                    def _(e, j=j):
                        return e.tensor_copy(out=xnfT[:, j * T:(j + 1) * T],
                                             in_=psT[:, :])

                @step("tensor", r=("xnTbf", "c_wxpT"), w=("psE",))
                def _(e):
                    r = None
                    for j2 in range(NDT):
                        r = e.matmul(psE[:, :],
                                     c_wxpT[:, j2 * 96:(j2 + 1) * 96],
                                     xnTbf[:, j2 * T:(j2 + 1) * T],
                                     start=(j2 == 0), stop=(j2 == NDT - 1))
                    return r

                @step("scalar", r=("psE",), w=("dtrawbf",))
                def _(e):
                    return e.activation(out=dtrawbf[:, :], in_=psE[0:64, :],
                                        func=AF.Copy)

                @step("vector", r=("psE",), w=("bcsb",))
                def _(e):
                    return e.tensor_copy(out=bcsb[:, :], in_=psE[64:96, :])

                @step("gpsimd", r=("bcsb",), w=(BCD,), n_dma=1)
                def _(e, p=p):
                    return e.dma_start(out=bcd.ap()[p], in_=bcsb[:, :])

                for j in range(NJ):
                    @step("tensor", r=("dtrawbf", "c_wdtT"), w=("psBm",))
                    def _(e, j=j):
                        return e.matmul(psBm[:, :],
                                        c_wdtT[:, j * 128:(j + 1) * 128],
                                        dtrawbf[:, :], start=True, stop=True)

                    @step("scalar", r=("psBm", "c_pack"), w=("zb",))
                    def _(e, j=j):
                        return e.activation(
                            out=zb[:, :], in_=psBm[:, :], func=AF.Identity,
                            bias=c_pack[:, BDT_ + j:BDT_ + j + 1])

                    @step("scalar", r=("zb", "c_pack"), w=("ebuf",))
                    def _(e):
                        e.activation(out=ebuf[:, :], in_=zb[:, :], func=AF.Abs)
                        e.activation(out=ebuf[:, :], in_=ebuf[:, :],
                                     func=AF.Exp,
                                     scale=c_pack[:, NEG1_:NEG1_ + 1])
                        return e.activation(out=ebuf[:, :], in_=ebuf[:, :],
                                            func=AF.Ln,
                                            bias=c_pack[:, ONE_:ONE_ + 1])

                    @step("vector", r=("zb", "ebuf", "xnfT"),
                          w=("dt4", "dtx4"))
                    def _(e, j=j):
                        e.scalar_tensor_tensor(
                            out=dt4[:, j * T:(j + 1) * T], in0=zb[:, :],
                            scalar=0.0, in1=ebuf[:, :],
                            op0=ALU.max, op1=ALU.add)
                        return e.tensor_tensor(
                            out=dtx4[:, j * T:(j + 1) * T],
                            in0=dt4[:, j * T:(j + 1) * T],
                            in1=xnfT[:, j * T:(j + 1) * T], op=ALU.mult)

                @step("vector", r=(), w=("yb",))
                def _(e):
                    return e.memset(yb[:, :], 0.0)

                for nq in range(NQ):
                    pq = nq % 2
                    BR, CR = f"brep{pq}", f"crep{pq}"

                    @step("gpsimd", r=(BCD,), w=(BR, CR), n_dma=2)
                    def _(e, nq=nq, p=p, pq=pq):
                        return [
                            e.dma_start(
                                out=view3(brep[pq], NQ),
                                in_=bcd.ap()[p, nq * 4:(nq + 1) * 4, :]
                                .unsqueeze(0).broadcast_to([128, 4, T])),
                            e.dma_start(
                                out=view3(crep[pq], NQ),
                                in_=bcd.ap()[p, 16 + nq * 4:20 + nq * 4, :]
                                .unsqueeze(0).broadcast_to([128, 4, T])),
                        ]

                    for j in range(NJ):
                        pa = (nq * NJ + j) % 2
                        AQ = f"aq{pa}"
                        aq3 = view3(aq[pa], NQ)

                        @step("scalar", r=("dt4", "c_acol"), w=(AQ,))
                        def _(e, nq=nq, j=j, aq3=aq3):
                            r = None
                            for k in range(4):
                                idx = j * NST + nq * 4 + k
                                r = e.activation(
                                    out=aq3[:, k, :],
                                    in_=dt4[:, j * T:(j + 1) * T],
                                    func=AF.Exp,
                                    scale=c_acol[:, idx:idx + 1])
                            return r

                        @step("vector", r=("dtx4", BR, AQ, "hst"),
                              w=("uq", "hq"))
                        def _(e, nq=nq, j=j, pq=pq, aq3=aq3):
                            e.tensor_tensor(
                                out=uq3,
                                in0=dtx4[:, j * T:(j + 1) * T].unsqueeze(1)
                                .broadcast_to([128, 4, T]),
                                in1=view3(brep[pq], NQ), op=ALU.mult)
                            r = None
                            for k in range(4):
                                idx = j * NST + nq * 4 + k
                                r = e.tensor_tensor_scan(
                                    out=hq3[:, k, :], data0=aq3[:, k, :],
                                    data1=uq3[:, k, :],
                                    initial=hst[:, idx:idx + 1],
                                    op0=ALU.mult, op1=ALU.add)
                            return r

                        @step("vector", r=("hq",), w=("hst",))
                        def _(e, nq=nq, j=j):
                            base = j * NST + nq * 4
                            return e.tensor_copy(out=hst[:, base:base + 4],
                                                 in_=hq3[:, :, T - 1:T])

                        @step("vector", r=("hq", CR), w=("uq", "tr", "zb", "yb"))
                        def _(e, nq=nq, j=j, pq=pq):
                            e.tensor_tensor(out=uq[:, :], in0=hq[:, :],
                                            in1=crep[pq][:, :], op=ALU.mult)
                            e.tensor_tensor(out=tr1[:, :], in0=uq3[:, 0, :],
                                            in1=uq3[:, 1, :], op=ALU.add)
                            e.tensor_tensor(out=tr2[:, :], in0=uq3[:, 2, :],
                                            in1=uq3[:, 3, :], op=ALU.add)
                            e.tensor_tensor(out=tr1[:, :], in0=tr1[:, :],
                                            in1=tr2[:, :], op=ALU.add)
                            return e.tensor_tensor(
                                out=yb[:, j * T:(j + 1) * T],
                                in0=yb[:, j * T:(j + 1) * T],
                                in1=tr1[:, :], op=ALU.add)

                @step("vector", r=("xnfT", "c_pack"), w=("yb",))
                def _(e):
                    r = None
                    for j in range(NJ):
                        r = e.scalar_tensor_tensor(
                            out=yb[:, j * T:(j + 1) * T],
                            in0=xnfT[:, j * T:(j + 1) * T],
                            scalar=c_pack[:, DPAR_ + j:DPAR_ + j + 1],
                            in1=yb[:, j * T:(j + 1) * T],
                            op0=ALU.mult, op1=ALU.add)
                    return r

                for g in range(G):
                    @step("tensor", r=("yb", "c_idn"), w=("psT",))
                    def _(e, g=g):
                        r = None
                        for jj in range(NJ):
                            r = e.transpose(
                                psT[:, jj * 128:(jj + 1) * 128],
                                yb[:, jj * T + g * 128:jj * T + (g + 1) * 128],
                                c_idn[:, :])
                        return r

                    @step("vector", r=("psT",), w=("xnsT",))
                    def _(e, g=g):
                        return e.tensor_copy(
                            out=xnsT[:, g * DS:(g + 1) * DS], in_=psT[:, :])

                @step("gpsimd", r=("xnsT",), w=("s_d",), n_dma=1)
                def _(e, c=c):
                    return e.dma_start(
                        out=s_d.ap()[c * T:(c + 1) * T, :]
                        .rearrange("(g p) d -> p g d", p=128),
                        in_=xnsT[:, :].rearrange("p (g d) -> p g d", g=G))

            @step("gpsimd", r=("s_d",), w=(), cc_inc=True)
            def _(e):
                return e.collective_compute(
                    "AllGather", ALU.bypass,
                    replica_groups=[[0, 1, 2, 3], [4, 5, 6, 7]],
                    ins=[s_d.ap().opt()], outs=[s_g.ap().opt()])

            barrier()

        with ExitStack() as fa:
            fsb = lambda n, s, d=F32: fa.enter_context(nc.sbuf_tensor(n, s, d))
            fxchH = [fsb(f"fxchH{i}", [128, 2 * D]) for i in range(2)]
            fstH = [fsb(f"fstH{i}", [128, 16]) for i in range(2)]
            fscr = fsb("fscr", [128, D])
            fxnbf = fsb("fxnbf", [128, G * D], BF16)
            xnT = [fsb(f"xnT{i}", [128, NDT * T], BF16) for i in range(2)]
            wgb = [fsb(f"wgb{i}", [128, NDT * 512], BF16) for i in range(2)]
            wub = [fsb(f"wub{i}", [128, NDT * 512], BF16) for i in range(2)]
            gb = [fsb(f"gb{i}", [128, 512]) for i in range(2)]
            fusebf = [fsb(f"fusebf{i}", [128, 512], BF16) for i in range(2)]

            fxnbf3 = fxnbf[:, :].rearrange("p (g d) -> p g d", g=G)

            for tc in range(FTC):
                p = tc % 2
                FXNT = f"xnT{p}"
                fxnT4 = xnT[p][:, :].rearrange("p (j gg t) -> p j gg t",
                                               j=NDT, gg=G)

                for hh in range(2):
                    XCHT, STT_ = f"fxchH{hh}", f"fstH{hh}"
                    fxch3 = fxchH[hh][:, :].rearrange("p (g d) -> p g d", g=2)

                    @step("sync", r=(), w=(XCHT,), n_dma=1)
                    def _(e, tc=tc, hh=hh, fxch3=fxch3):
                        lo = tc * T + hh * 256
                        return e.dma_start(out=fxch3,
                                           in_=xq[lo:lo + 256, :]
                                           .rearrange("(g p) d -> p g d",
                                                      p=128))

                    _ln_steps(step, "f", XCHT, STT_, "fscr", fxch3, fstH[hh],
                              fscr, c_pack, 2)

                    @step("scalar", r=(XCHT, STT_), w=("fxnbf",))
                    def _(e, hh=hh, fxch3=fxch3):
                        r = None
                        for gg in range(2):
                            g = 2 * hh + gg
                            r = e.activation(out=fxnbf3[:, g, :],
                                             in_=fxch3[:, gg, :],
                                             func=AF.Identity,
                                             scale=fstH[hh][:, 10 + gg:11 + gg],
                                             bias=fstH[hh][:, 12 + gg:13 + gg])
                        return r

                @step("sync", r=("fxnbf",), w=(FXNT,), n_dma=4)
                def _(e, fxnT4=fxnT4):
                    return [
                        e.dma_start(out=fxnT4[:, :, g, :],
                                    in_=fxnbf3[:, g, :]
                                    .rearrange("t (j p) -> t j p", j=NDT),
                                    transpose=True)
                        for g in range(G)
                    ]

                for sp in range(ISP):
                    pw = sp % 2
                    WGB, WUB = f"wgb{pw}", f"wub{pw}"
                    wgb3 = wgb[pw][:, :].rearrange("p (j i) -> p j i", j=NDT)
                    wub3 = wub[pw][:, :].rearrange("p (j i) -> p j i", j=NDT)

                    @step("gpsimd", r=(), w=(WGB, WUB), n_dma=2)
                    def _(e, sp=sp, wgb3=wgb3, wub3=wub3):
                        return [
                            e.dma_start(out=wgb3,
                                        in_=wg[:, sp * 512:(sp + 1) * 512]
                                        .rearrange("(j p) i -> p j i", p=128)),
                            e.dma_start(out=wub3,
                                        in_=wu[:, sp * 512:(sp + 1) * 512]
                                        .rearrange("(j p) i -> p j i", p=128)),
                        ]

                    for it in range(4):
                        pg = it % 2
                        PG, PU = f"psG{pg}", f"psU{pg}"
                        GB, FUB = f"gb{pg}", f"fusebf{pg}"

                        @step("tensor", r=(WGB, WUB, FXNT), w=(PG, PU))
                        def _(e, it=it, p=p, pg=pg, wgb3=wgb3, wub3=wub3):
                            r = None
                            for j in range(NDT):
                                e.matmul(psG[pg][:, :],
                                         wgb3[:, j, it * 128:(it + 1) * 128],
                                         xnT[p][:, j * T:(j + 1) * T],
                                         start=(j == 0), stop=(j == NDT - 1))
                            for j in range(NDT):
                                r = e.matmul(
                                    psU[pg][:, :],
                                    wub3[:, j, it * 128:(it + 1) * 128],
                                    xnT[p][:, j * T:(j + 1) * T],
                                    start=(j == 0), stop=(j == NDT - 1))
                            return r

                        @step("scalar", r=(PG, "c_pack"), w=(GB,))
                        def _(e, sp=sp, it=it, pg=pg):
                            col = SG_ + sp * 4 + it
                            return e.activation(out=gb[pg][:, :],
                                                in_=psG[pg][:, :],
                                                func=AF.Silu,
                                                scale=c_pack[:, col:col + 1])

                        @step("vector", r=(PU, GB, "c_pack"), w=(FUB,))
                        def _(e, sp=sp, it=it, pg=pg):
                            col = SU_ + sp * 4 + it
                            return e.scalar_tensor_tensor(
                                out=fusebf[pg][:, :], in0=psU[pg][:, :],
                                scalar=c_pack[:, col:col + 1],
                                in1=gb[pg][:, :],
                                op0=ALU.mult, op1=ALU.mult)

                        @step("gpsimd", r=(FUB,), w=("fuse_d",), n_dma=1)
                        def _(e, sp=sp, it=it, tc=tc, pg=pg):
                            row = (sp * 4 + it) * 128
                            return e.dma_start(
                                out=fuse_d.ap()[row:row + 128,
                                                tc * T:(tc + 1) * T],
                                in_=fusebf[pg][:, :])

            barrier()

        with ExitStack() as fb:
            fsb2 = lambda n, s, d=F32: fb.enter_context(
                nc.sbuf_tensor(n, s, d))
            fusesb = fsb2("fusesb", [128, NIT * T], BF16)
            wdb = [fsb2(f"wdb{i}", [128, NIT * 128], BF16) for i in range(2)]
            gbuf = fsb2("gbuf", [128, NDT * T])
            gTt = fsb2("gTt", [128, D])
            ssel = fsb2("ssel", [128, 16 * DS])
            sacc = fsb2("sacc", [128, D])
            xt = fsb2("xt", [128, D])

            fusesb3 = fusesb[:, :].rearrange("p (it t) -> p it t", it=NIT)
            gbuf3 = gbuf[:, :].rearrange("p (j t) -> p j t", j=NDT)
            ssel4 = ssel[:, :].rearrange("p (r q d) -> p r q d", r=4, q=4)
            sgv = s_g.ap().rearrange("(r q c gg p) d -> p r q c gg d",
                                     r=4, q=4, c=FTC, gg=G, p=128)

            for tc in range(FTC):
                @step("sync", r=("fuse_d",), w=("fusesb",), n_dma=1)
                def _(e, tc=tc):
                    return e.dma_start(
                        out=fusesb3,
                        in_=fuse_d.ap()[:, tc * T:(tc + 1) * T]
                        .rearrange("(it p) t -> p it t", p=128))

                for dt in range(NDT):
                    pw = dt % 2
                    WDB = f"wdb{pw}"
                    wdb3 = wdb[pw][:, :].rearrange("p (it d) -> p it d",
                                                   it=NIT)

                    @step("gpsimd", r=(), w=(WDB,), n_dma=1)
                    def _(e, dt=dt, pw=pw):
                        return e.dma_start(out=wdb[pw][:, :],
                                           in_=wdp[dt, :, :])

                    @step("tensor", r=(WDB, "fusesb"), w=("psC",))
                    def _(e, dt=dt, wdb3=wdb3):
                        r = None
                        for it in range(NIT):
                            r = e.matmul(psC[:, :], wdb3[:, it, :],
                                         fusesb3[:, it, :],
                                         start=(it == 0),
                                         stop=(it == NIT - 1))
                        return r

                    @step("scalar", r=("psC", "c_pack"), w=("gbuf",))
                    def _(e, dt=dt):
                        col = SD_ + dt
                        return e.activation(out=gbuf3[:, dt, :],
                                            in_=psC[:, :], func=AF.Silu,
                                            scale=c_pack[:, col:col + 1])

                for g in range(G):
                    @step("sync", r=(), w=("ssel", "xt"), n_dma=2,
                          cc_wait=True)
                    def _(e, tc=tc, g=g):
                        return [
                            e.dma_start(out=ssel4,
                                        in_=sgv[:, :, :, tc, g, :]),
                            e.dma_start(out=xt[:, :],
                                        in_=xq[tc * T + g * 128:
                                               tc * T + (g + 1) * 128, :]),
                        ]

                    for rr in range(4):
                        @step("tensor", r=("gbuf", "c_idn"), w=("psT",))
                        def _(e, g=g, rr=rr):
                            r = None
                            for u in range(4):
                                dt = rr * 4 + u
                                r = e.transpose(
                                    psT[:, u * 128:(u + 1) * 128],
                                    gbuf[:, dt * T + g * 128:
                                         dt * T + (g + 1) * 128],
                                    c_idn[:, :])
                            return r

                        @step("vector", r=("psT",), w=("gTt",))
                        def _(e, rr=rr):
                            return e.tensor_copy(
                                out=gTt[:, rr * 512:(rr + 1) * 512],
                                in_=psT[:, :])

                    @step("vector", r=("ssel", "c_pack"), w=("sacc",))
                    def _(e):
                        r = None
                        for r4 in range(4):
                            e.tensor_single_scalar(
                                sacc[:, r4 * DS:(r4 + 1) * DS],
                                ssel4[:, r4, 0, :],
                                c_pack[:, MASK_:MASK_ + 1], ALU.mult)
                            for Q in range(1, 4):
                                r = e.scalar_tensor_tensor(
                                    out=sacc[:, r4 * DS:(r4 + 1) * DS],
                                    in0=ssel4[:, r4, Q, :],
                                    scalar=c_pack[:, MASK_ + Q:MASK_ + Q + 1],
                                    in1=sacc[:, r4 * DS:(r4 + 1) * DS],
                                    op0=ALU.mult, op1=ALU.add)
                        return r

                    @step("vector", r=("gTt", "xt", "sacc"), w=("sacc",))
                    def _(e):
                        e.tensor_tensor(out=sacc[:, :], in0=sacc[:, :],
                                        in1=gTt[:, :], op=ALU.mult)
                        return e.tensor_tensor(out=sacc[:, :], in0=sacc[:, :],
                                               in1=xt[:, :], op=ALU.add)

                    @step("gpsimd", r=("sacc",), w=(), n_dma=1)
                    def _(e, tc=tc, g=g):
                        return e.dma_start(
                            out=outq[tc * T + g * 128:
                                     tc * T + (g + 1) * 128, :],
                            in_=sacc[:, :])

            barrier()

        engcnt = {e: 0 for e in ENGS}
        last_w = {}
        readers = {}
        stepidx = {e: 0 for e in ENGS}
        plans = []

        for s in steps:
            waits = {}

            def add(eng2, c2):
                if waits.get(eng2, 0) < c2:
                    waits[eng2] = c2

            if s.barrier:
                for e2 in ENGS:
                    if e2 != s.eng and engcnt[e2] > 0:
                        add(e2, engcnt[e2])
            else:
                for t in s.r:
                    if t in last_w:
                        e2, c2, si2, wdma = last_w[t]
                        if e2 != s.eng:
                            add(e2, c2)
                        elif wdma or stepidx[s.eng] - si2 <= 2:
                            add(e2, c2)
                for t in s.w:
                    if t in last_w:
                        e2, c2, si2, wdma = last_w[t]
                        if e2 != s.eng:
                            add(e2, c2)
                        elif wdma:
                            add(e2, c2)
                    for e2, c2 in readers.get(t, {}).items():
                        if e2 != s.eng:
                            add(e2, c2)

            plans.append(waits)
            engcnt[s.eng] += s.inc
            si = stepidx[s.eng]
            stepidx[s.eng] += 1
            if not s.barrier:
                for t in s.w:
                    last_w[t] = (s.eng, engcnt[s.eng], si, s.dma)
                    readers[t] = {}
                for t in s.r:
                    readers.setdefault(t, {})[s.eng] = engcnt[s.eng]

        totals = dict(engcnt)

        with nc.Block() as block:
            def make_body(engname):
                def body(eng):
                    watermark = {e2: 0 for e2 in ENGS}
                    for s, waits in zip(steps, plans):
                        if s.eng != engname:
                            continue
                        for e2 in ENGS:
                            c2 = waits.get(e2, 0)
                            if c2 > watermark[e2]:
                                eng.wait_ge(semE[e2], c2)
                                watermark[e2] = c2
                        if s.cc_wait:
                            eng.wait_ge(semC, 1)
                        if s.barrier:
                            eng.nop().then_inc(semE[engname], 1)
                            continue
                        ret = s.fn(eng)
                        if s.cc_inc:
                            ret.then_inc(semC, 1)
                        elif s.dma:
                            instrs = ret if isinstance(ret, list) else [ret]
                            assert 16 * len(instrs) == s.inc
                            for ins_ in instrs:
                                ins_.then_inc(semE[engname], 16)
                        else:
                            ret.then_inc(semE[engname], 1)
                    for e2 in ENGS:
                        if totals[e2] > watermark[e2]:
                            eng.wait_ge(semE[e2], totals[e2])
                return body

            block.sync(make_body("sync"))
            block.gpsimd(make_body("gpsimd"))
            block.scalar(make_body("scalar"))
            block.vector(make_body("vector"))
            block.tensor(make_body("tensor"))

    return nc


def kernel(L0_attn_x, L0_attn_W_xproj, L0_attn_W_dt, L0_attn_b_dt,
           L0_attn_A_log, L0_attn_D_param, L0_ffn_w_gate, L0_ffn_w_up,
           L0_ffn_w_down, L0_ffn_s_gate, L0_ffn_s_up, L0_ffn_s_down):
    x = np.asarray(L0_attn_x, dtype=np.float32)
    W_xproj = np.asarray(L0_attn_W_xproj, dtype=np.float32)
    W_dt = np.asarray(L0_attn_W_dt, dtype=np.float32)
    b_dt = np.asarray(L0_attn_b_dt, dtype=np.float32)
    A_log = np.asarray(L0_attn_A_log, dtype=np.float32)
    D_param = np.asarray(L0_attn_D_param, dtype=np.float32)
    w_gate = np.asarray(L0_ffn_w_gate)
    w_up = np.asarray(L0_ffn_w_up)
    w_down = np.asarray(L0_ffn_w_down)
    s_gate = np.asarray(L0_ffn_s_gate, dtype=np.float32)
    s_up = np.asarray(L0_ffn_s_up, dtype=np.float32)
    s_down = np.asarray(L0_ffn_s_down, dtype=np.float32)

    nc = build_program()

    wxpT = np.ascontiguousarray(W_xproj.T)
    wg = np.ascontiguousarray(w_gate)
    wu = np.ascontiguousarray(w_up)
    wdp = np.ascontiguousarray(
        w_down.reshape(64, 128, 16, 128).transpose(2, 1, 0, 3)
    ).reshape(16, 128, INTER)
    idn = np.eye(128, dtype=np.float32)
    sg_c = np.ascontiguousarray(s_gate.reshape(64, 128).T)
    su_c = np.ascontiguousarray(s_up.reshape(64, 128).T)
    sd_c = np.ascontiguousarray(s_down.reshape(16, 128).T)

    in_maps = []
    for k in range(8):
        b, r = k // 4, k % 4
        dlo = r * DS
        cp = np.zeros((128, CPW), dtype=np.float32)
        cp[:, EPS_] = EPS
        cp[:, ONE_] = 1.0
        cp[:, NEG1_] = -1.0
        cp[:, BDT_:BDT_ + 4] = b_dt[dlo:dlo + DS].reshape(4, 128).T
        cp[:, ALOG_:ALOG_ + 64] = (
            A_log[dlo:dlo + DS].reshape(4, 128, 16).transpose(1, 0, 2)
            .reshape(128, 64))
        cp[:, DPAR_:DPAR_ + 4] = D_param[dlo:dlo + DS].reshape(4, 128).T
        cp[:, SG_:SG_ + 64] = sg_c
        cp[:, SU_:SU_ + 64] = su_c
        cp[:, SD_:SD_ + 16] = sd_c
        cp[:, MASK_ + r] = 1.0
        in_maps.append(dict(
            xb=np.ascontiguousarray(x[b]),
            xbs=np.ascontiguousarray(x[b][:, dlo:dlo + DS]),
            xq=np.ascontiguousarray(x[b, r * TQ:(r + 1) * TQ]),
            wxpT=wxpT,
            wdtT=np.ascontiguousarray(W_dt[dlo:dlo + DS].T),
            cpack=cp,
            idn=idn,
            wg=wg,
            wu=wu,
            wdp=wdp,
        ))

    res = run_bass_kernel_spmd(nc, in_maps, core_ids=list(range(8)))

    out = np.empty((2, L, D), dtype=np.float32)
    for k in range(8):
        b, r = k // 4, k % 4
        out[b, r * TQ:(r + 1) * TQ] = np.asarray(res.results[k]["outq"])
    return out
